# version 07
# baseline (speedup 1.0000x reference)
import contextlib
import numpy as np

import concourse.bass as bass
import concourse.mybir as mybir
from concourse.bass_utils import run_bass_kernel_spmd

F32 = mybir.dt.float32
BF16 = mybir.dt.bfloat16
AF = mybir.ActivationFunctionType
ALU = mybir.AluOpType

P = 128
D = 1024
KD = 8
S = 4096
SC = 512
NCH = S // SC
NJ = SC // P

C1 = 0.7978845608028654
C3 = C1 * 0.044715
EPS = 1e-5

SEMS = ("cp", "cwT", "g1", "g2", "gsp", "xld", "x2", "p2g", "p2n", "p2a",
        "sxg", "sxnb", "sa1", "yst", "pe", "act", "dve", "gp")
OWN = {"scalar": "act", "vector": "dve", "tensor": "pe", "gpsimd": "gp"}


class Plan:
    def __init__(self):
        self.q = {n: [] for n in ("sync", "scalar", "vector", "tensor", "gpsimd")}
        self.cnt = {s: 0 for s in SEMS}
        self.ev = {}
        self.skipped = []

    def emit(self, qn, fns, deps=(), inc=None, label=None):
        if not isinstance(fns, (list, tuple)):
            fns = [fns]
        ws = {}
        for d in deps:
            if d is None:
                continue
            if isinstance(d, tuple) and len(d) == 3 and d[0] == "#":
                s, c = d[1], d[2]
            else:
                if d not in self.ev:
                    self.skipped.append(d)
                    continue
                s, c = self.ev[d]
            if c > ws.get(s, 0):
                ws[s] = c
        if inc is not None:
            s, n = inc
            self.cnt[s] += n
            if label is not None:
                self.ev[label] = (s, self.cnt[s])
        self.q[qn].append((list(fns), ws, inc, label))

    def run(self, qn, eng, sems):
        hw = {}
        own = OWN.get(qn)
        for fns, ws, inc, _lb in self.q[qn]:
            for s in SEMS:
                c = ws.get(s)
                if c is None or s == own or hw.get(s, 0) >= c:
                    continue
                eng.wait_ge(sems[s], c)
                hw[s] = c
            inst = None
            for f in fns:
                inst = f(eng)
            if inc is not None and inst is not None:
                inst.then_inc(sems[inc[0]], inc[1])


def build_nc(detect_races=True, dbg=False):
    nc = bass.Bass("TRN2", target_bir_lowering=False,
                   detect_race_conditions=detect_races)

    x_h = nc.dram_tensor("x", [S, D], F32, kind="ExternalInput").ap()
    w_h = {}
    for nm in ("wf1", "wf2", "wbx", "wgx", "wga", "wb2", "wT"):
        w_h[nm] = nc.dram_tensor(nm, [D, D], F32, kind="ExternalInput").ap()
    dna_h = nc.dram_tensor("dna", [100, D], F32, kind="ExternalInput").ap()
    uaT_h = nc.dram_tensor("uaT", [50, D], F32, kind="ExternalInput").ap()
    ldT_h = nc.dram_tensor("ldT", [100, 4], F32, kind="ExternalInput").ap()
    luT_h = nc.dram_tensor("luT", [4, 50], F32, kind="ExternalInput").ap()
    packs_h = nc.dram_tensor("packs", [P, 72], F32, kind="ExternalInput").ap()
    afwbc_h = nc.dram_tensor("afwbc", [P, KD * SC], F32, kind="ExternalInput").ap()
    ident_h = nc.dram_tensor("ident", [P, P], F32, kind="ExternalInput").ap()
    y_h = nc.dram_tensor("y", [S, D], F32, kind="ExternalOutput").ap()
    KND = "ExternalOutput" if dbg else "Internal"
    a1_h = nc.dram_tensor("a1s", [P, NCH, KD * SC], F32, kind=KND).ap()
    xg_h = nc.dram_tensor("xgs", [P, NCH, KD * SC], BF16, kind=KND).ap()
    xnb_h = nc.dram_tensor("xnbs", [P, NCH, KD * SC], BF16, kind=KND).ap()
    wc_h = nc.dram_tensor("wcs", [P, KD, D], BF16, kind=KND).ap()
    if dbg:
        dbg_h = nc.dram_tensor("dbgs", [P, NCH, 64], F32, kind="ExternalOutput").ap()

    x_r = x_h.rearrange("(c j p) d -> c p j d", p=P, j=NJ)
    y_r = y_h.rearrange("(c j p) d -> c p j d", p=P, j=NJ)
    w_r = {nm: w_h[nm].rearrange("(k p) n -> k p n", p=P) for nm in w_h}

    align = lambda o: (o + 31) & ~31
    base = align(nc.sbuf_base)
    off = base

    def at(name, shape, dtype, o):
        return nc.alloc_sbuf_tensor_at(name, shape, dtype, offset=o)

    def persist(name, shape, dtype):
        nonlocal off
        t = at(name, shape, dtype, off)
        nbytes = int(np.prod(shape[1:])) * (4 if dtype == F32 else 2)
        off = align(off + nbytes)
        return t

    ident = persist("identt", [P, P], F32)
    q25 = persist("q25c", [P, 1], F32)
    packs = persist("packs_s", [P, 72], F32)
    dhalf = persist("dhalf", [P, 72], F32)
    dwh = persist("dwh", [P, 72], F32)
    carry = persist("carryb", [P, KD * 64], F32)
    mvm = [persist(f"mvm{i}", [P, 64], F32) for i in range(2)]
    ssq = [persist(f"ssq{i}", [P, 64], F32) for i in range(2)]
    smv = [persist(f"smv{i}", [P, 64], F32) for i in range(2)]
    tmp4 = [persist(f"tmp4{i}", [P, 64], F32) for i in range(2)]
    rstd = [persist(f"rstd{i}", [P, 64], F32) for i in range(2)]

    region = align(off)
    TOP = nc.sbuf_top

    o = region
    def ph(name, shape, dtype):
        nonlocal o
        t = at(name, shape, dtype, o)
        nbytes = int(np.prod(shape[1:])) * (4 if dtype == F32 else 2)
        o = align(o + nbytes)
        return t

    xbuf = [ph(f"xb{i}", [P, NJ, D], F32) for i in range(2)]
    sc_xn = [ph(f"scxn{i}", [P, SC], F32) for i in range(2)]
    sc_a = [ph(f"sca{i}", [P, SC], F32) for i in range(2)]
    sc_b = [ph(f"scb{i}", [P, SC], F32) for i in range(2)]
    xg = [ph(f"xg{i}", [P, KD * SC], BF16) for i in range(2)]
    xnbb = ph("xnbb", [P, KD * SC], BF16)
    vb = ph("vb", [P, KD * SC], BF16)
    thx = ph("thx", [P, KD * SC], F32)
    abuf = ph("abuf", [P, KD * SC], F32)
    hbb = ph("hbb", [P, KD * SC], BF16)
    a1b = ph("a1b", [P, KD * SC], F32)
    wP1 = {nm: ph(f"w_{nm}", [P, KD, D], BF16)
           for nm in ("wbx", "wgx", "wga", "wb2")}
    junk = ph("junk", [P, D], F32)
    assert o <= TOP, f"phase1 sbuf overflow {o} > {TOP}"

    so = region
    so = None
    a1b_off = None
    a1b_off = nc.lookup_mloc(a1b).addr
    so = a1b_off
    def su(name, shape, dtype):
        nonlocal so
        t = at(name, shape, dtype, so)
        nbytes = int(np.prod(shape[1:])) * (4 if dtype == F32 else 2)
        so = align(so + nbytes)
        return t
    t1b = su("t1b", [4, D], BF16)
    t2b = su("t2b", [50, D], BF16)
    ldTb = su("ldTb", [100, 4], BF16)
    luTb = su("luTb", [4, 50], BF16)
    dnab = su("dnab", [100, D], BF16)
    uaTb = su("uaTb", [50, D], BF16)
    assert so <= a1b_off + KD * SC * 4

    o2 = region
    def p2(name, shape, dtype):
        nonlocal o2
        t = at(name, shape, dtype, o2)
        nbytes = int(np.prod(shape[1:])) * (4 if dtype == F32 else 2)
        o2 = align(o2 + nbytes)
        return t

    xgin = [p2(f"xgin{i}", [P, KD * SC], BF16) for i in range(2)]
    xnbin = [p2(f"xnbin{i}", [P, KD * SC], BF16) for i in range(2)]
    a1in = [p2(f"a1in{i}", [P, KD * SC], F32) for i in range(2)]
    xin = [p2(f"xin{i}", [P, NJ, D], F32) for i in range(2)]
    ubuf = p2("ubuf", [P, KD * SC], F32)
    afw = p2("afw", [P, KD * SC], F32)
    hfb = p2("hfb", [P, KD * SC], BF16)
    gbuf = p2("gbuf", [P, KD * SC], F32)
    wP2 = {nm: p2(f"w2_{nm}", [P, KD, D], BF16) for nm in ("wf1", "wf2", "wc")}
    assert o2 <= TOP, f"phase2 sbuf overflow {o2} > {TOP}"

    pb = [nc.alloc_psum_tensor(f"pb{i}", [P, SC], F32) for i in range(8)]

    pl = Plan()
    E = pl.emit
    ev = pl.ev

    def pk(v, k):
        return packs[:, v * KD + k:v * KD + k + 1]
    BF1, BF2, BBX, BGX, BGA, BB2, BC, AFW, APR = range(9)
    def hbgx_c(k):
        return dhalf[:, BGX * KD + k:BGX * KD + k + 1]
    def hbga_c(k):
        return dhalf[:, BGA * KD + k:BGA * KD + k + 1]
    def whalf_c(k):
        return dwh[:, APR * KD + k:APR * KD + k + 1]

    vq = lambda q: 0 <= q < NCH

    E("sync", lambda e: e.dma_start(out=packs[:], in_=packs_h),
      inc=("cp", 16))
    E("sync", lambda e: e.dma_start(out=ident[:], in_=ident_h),
      inc=("cp", 16), label="Lcp")
    for nm in ("wbx", "wgx", "wga", "wb2"):
        for k in range(KD):
            E("gpsimd",
              (lambda nm=nm, k=k: lambda e: e.dma_start(
                  out=wP1[nm][:, k, :], in_=w_r[nm][k]))(),
              inc=("g1", 16), label=("wld", nm, k))
    for nm, dst, src in (("ldT", ldTb, ldT_h), ("luT", luTb, luT_h),
                         ("dna", dnab, dna_h), ("uaT", uaTb, uaT_h)):
        E("gpsimd", (lambda dst=dst, src=src: lambda e: e.dma_start(
            out=dst[:], in_=src))(), inc=("g1", 16), label=("wld", nm))
    for k in range(KD):
        dst = thx if k < 4 else abuf
        kk = k % 4
        E("sync", (lambda dst=dst, kk=kk, k=k: lambda e: e.dma_start(
            out=dst[:, kk * D:(kk + 1) * D], in_=w_r["wT"][k]))(),
          inc=("cwT", 16), label=("wTld", k))

    G1TOT = ("#", "g1", pl.cnt["g1"])
    CWTOT = ("#", "cwT", pl.cnt["cwT"])
    E("scalar", lambda e: e.activation(dwh[:], packs[:], AF.Exp),
      deps=["Lcp"], inc=("act", 1), label="su1")
    E("vector", lambda e: e.tensor_scalar(dwh[:], dwh[:], 1.0, None, ALU.add),
      deps=["su1"], inc=("dve", 1), label="su2")
    E("scalar", lambda e: e.activation(dwh[:], dwh[:], AF.Ln),
      deps=["su2"], inc=("act", 1), label="su3")
    E("vector", lambda e: e.tensor_scalar(dwh[:], dwh[:], -4.0, None, ALU.mult),
      deps=["su3"], inc=("dve", 1), label="su4")
    E("vector", lambda e: e.tensor_scalar(dhalf[:], packs[:], 0.5, None, ALU.mult),
      deps=["Lcp"], inc=("dve", 1), label="su5")
    E("vector", lambda e: e.memset(q25[:], 0.25),
      deps=[], inc=("dve", 1), label="q25")
    for bi, buf in enumerate((smv[0], smv[1], ssq[0], ssq[1])):
        E("vector", (lambda buf=buf: lambda e: e.tensor_scalar(
            buf[:], ident[:, 0:64], 0.0, None, ALU.mult))(),
          deps=["Lcp"], inc=("dve", 1), label=("szero", bi))

    for h in range(2):
        E("tensor", (lambda h=h: lambda e: e.matmul(
            pb[6 + h][0:4, :], ldTb[:], dnab[:, h * SC:(h + 1) * SC],
            start=True, stop=True))(),
          deps=[G1TOT], inc=("pe", 1),
          label=("suT1", h))
        E("vector", (lambda h=h: lambda e: e.tensor_copy(
            t1b[:, h * SC:(h + 1) * SC], pb[6 + h][0:4, :]))(),
          deps=[("suT1", h)], inc=("dve", 1), label=("suT1e", h))
    for h in range(2):
        E("tensor", (lambda h=h: lambda e: e.matmul(
            pb[6 + h][0:50, :], luTb[:], t1b[:, h * SC:(h + 1) * SC],
            start=True, stop=True))(),
          deps=[G1TOT, ("suT1e", 0), ("suT1e", 1)], inc=("pe", 1),
          label=("suT2", h))
        E("vector", (lambda h=h: lambda e: e.tensor_copy(
            t2b[:, h * SC:(h + 1) * SC], pb[6 + h][0:50, :]))(),
          deps=[("suT2", h)], inc=("dve", 1), label=("suT2e", h))
    for half in range(2):
        for it4 in range(4):
            it = half * 4 + it4
            wtsrc = thx if it < 4 else abuf
            wslice = (it % 4)
            for h in range(2):
                E("tensor", (lambda it=it, h=h: lambda e: e.matmul(
                    pb[6 + h][:], t2b[:, it * P:(it + 1) * P],
                    uaTb[:, h * SC:(h + 1) * SC], start=True, stop=True))(),
                  deps=[G1TOT, CWTOT, ("suT2e", 0), ("suT2e", 1),
                        ("suWc", half, it4 - 1, h) if it4 > 0
                        else ("suWc", half - 1, 3, h)],
                  inc=("pe", 1), label=("suL", it, h))
                E("vector", (lambda it=it, h=h, wtsrc=wtsrc, ws=wslice,
                             half=half, it4=it4: lambda e:
                             e.scalar_tensor_tensor(
                                 vb[:].rearrange("p (a n) -> p a n", a=8)
                                 [:, it4 * 2 + h, :],
                                 pb[6 + h][:], 1.0,
                                 wtsrc[:, ws * D + h * SC: ws * D + (h + 1) * SC],
                                 ALU.mult, ALU.add))(),
                  deps=[("suL", it, h)] +
                       ([("#", "gsp", 16)] if half == 1 else []),
                  inc=("dve", 1),
                  label=("suWc", half, it4, h))
        E("gpsimd", (lambda half=half: lambda e: e.dma_start(
            out=wc_h[:, half * 4:(half + 1) * 4, :].rearrange("p a n -> p (a n)"),
            in_=vb[:]))(),
          deps=[("suWc", half, 3, 1), ("suSp", half - 1)],
          inc=("gsp", 16), label=("suSp", half))
    setup_dve = ("#", "dve", pl.cnt["dve"])
    setup_gsp = ("#", "gsp", pl.cnt["gsp"])
    setup_pe = ("#", "pe", pl.cnt["pe"])

    psT = (pb[0], pb[1], pb[2], pb[3])
    psG = (pb[0], pb[1], pb[2], pb[3])
    psV = (pb[4], pb[5], pb[6], pb[7])
    psB = (pb[4], pb[5], pb[6], pb[7])

    def cdup(q):
        return NCH - 1 - q

    def ksl(buf, k):
        return buf[:, k * SC:(k + 1) * SC]

    for i in range(-2, NCH + 2):
        qx = i + 2
        if vq(qx):
            E("sync", (lambda qx=qx: lambda e: e.dma_start(
                out=xbuf[qx % 2][:], in_=x_r[cdup(qx)]))(),
              deps=[("T", qx - 2, 7), ("xld", qx - 1)],
              inc=("xld", 16), label=("xld", qx))

        qt = i + 1
        qo = i - 2
        for k in range(KD):
            if vq(qt):
                fns = [(lambda qt=qt, k=k, j=j: lambda e: e.transpose(
                    psT[k % 4][:, j * P:(j + 1) * P],
                    xbuf[qt % 2][:, j, k * P:(k + 1) * P], ident[:]))()
                    for j in list(range(NJ)) + [NJ - 1]]
                E("tensor", fns,
                  deps=[("nrm", qt, NJ - 1), ("Tev3", qt, k - 4),
                        ("ga1", qt - 2, (k % 4) + 4), "Lcp"],
                  inc=("pe", 1), label=("T", qt, k))
            if vq(qo):
                fns = [(lambda qo=qo, k=k, kk=kk: lambda e: e.matmul(
                    psB[k % 4][:], wP1["wb2"][:, kk, k * P:(k + 1) * P],
                    ksl(hbb, kk), start=(kk == 0), stop=(kk == KD - 1)))()
                    for kk in range(KD)]
                E("tensor", fns,
                  deps=[("hb", qo), ("a1", qo, k - 4), ("vev", qo + 1, (k % 4) + 4),
                        G1TOT, setup_dve],
                  inc=("pe", 1), label=("OB", qo, k))
                E("vector", (lambda qo=qo, k=k: lambda e: e.tensor_scalar(
                    ksl(a1b, k), psB[k % 4][:], pk(BB2, k), None, ALU.add))(),
                  deps=[("OB", qo, k), ("si", qo + 1), ("spA1", qo - 1)],
                  inc=("dve", 1), label=("a1", qo, k))
            if vq(qt):
                E("scalar", (lambda qt=qt, k=k: lambda e: e.activation(
                    sc_xn[k % 2][:], psT[k % 4][:], AF.Identity, scale=0.5))(),
                  deps=[("T", qt, k), ("gt", qt, k - 2), ("xg", qt, k - 2)],
                  inc=("act", 1), label=("Tev2", qt, k))
                E("scalar", (lambda qt=qt, k=k: lambda e: e.activation(
                    ksl(xnbb, k), psT[k % 4][:], AF.Identity))(),
                  deps=[("spxnb", qt - 1)],
                  inc=("act", 1), label=("Tev3", qt, k))
                E("gpsimd", (lambda k=k: lambda e: e.tensor_tensor(
                    sc_a[k % 2][:], sc_xn[k % 2][:], sc_xn[k % 2][:],
                    ALU.mult))(),
                  deps=[("Tev2", qt, k), ("th", qt, k - 2),
                        ("aw", qt - 2, 6 + (k % 2))],
                  inc=("gp", 1), label=("gsq", qt, k))
                E("gpsimd", (lambda k=k: lambda e: e.tensor_scalar(
                    sc_a[k % 2][:], sc_a[k % 2][:], 8.0 * C3, 2.0 * C1,
                    ALU.mult, ALU.add))(),
                  deps=[], inc=("gp", 1), label=("gq", qt, k))
                E("gpsimd", (lambda k=k: lambda e: e.tensor_tensor(
                    sc_a[k % 2][:], sc_a[k % 2][:], sc_xn[k % 2][:],
                    ALU.mult))(),
                  deps=[], inc=("gp", 1), label=("gt", qt, k))
                E("scalar", (lambda k=k: lambda e: e.activation(
                    sc_b[k % 2][:], sc_a[k % 2][:], AF.Tanh))(),
                  deps=[("gt", qt, k), ("xg", qt, k - 2)],
                  inc=("act", 1), label=("th", qt, k))
                E("vector", (lambda qt=qt, k=k: lambda e: e.scalar_tensor_tensor(
                    ksl(xg[qt % 2], k), sc_b[k % 2][:], 1.0, sc_xn[k % 2][:],
                    ALU.add, ALU.mult))(),
                  deps=[("th", qt, k), ("V", qt - 2, 7), ("spxg", qt - 2)],
                  inc=("dve", 1), label=("xg", qt, k))

        if vq(qx):
            par = qx % 2
            for j in range(NJ):
                E("gpsimd", (lambda qx=qx, j=j: lambda e: e.tensor_tensor(
                    junk[:], xbuf[qx % 2][:, j, :], xbuf[qx % 2][:, j, :],
                    ALU.mult))(),
                  deps=[("xld", qx), ("ssq2", qx, j - 1), ("ssq2", qx - 2, NJ - 1),
                        ("agV", qx - 2)],
                  inc=("gp", 1), label=("sqa2", qx, j))
                E("vector", (lambda qx=qx, j=j, par=par: lambda e: e.tensor_reduce(
                    smv[par][:, j:j + 1], xbuf[qx % 2][:, j, :],
                    mybir.AxisListType.X, ALU.add))(),
                  deps=[("xld", qx)], inc=("dve", 1), label=("smv2", qx, j))
                E("vector", (lambda qx=qx, j=j, par=par: lambda e: e.tensor_reduce(
                    ssq[par][:, j:j + 1], junk[:],
                    mybir.AxisListType.X, ALU.add))(),
                  deps=[("sqa2", qx, j)], inc=("dve", 1), label=("ssq2", qx, j))
            E("vector", (lambda par=par: lambda e: e.tensor_scalar(
                mvm[par][:], smv[par][:], 1.0 / 1024.0, None, ALU.mult))(),
              deps=[], inc=("dve", 1), label=("agM", qx))
            E("vector", (lambda par=par: lambda e: e.tensor_tensor(
                tmp4[par][:], mvm[par][:], mvm[par][:], ALU.mult))(),
              deps=[], inc=("dve", 1), label=("agE", qx))
            E("vector", (lambda par=par, qx=qx: lambda e: e.scalar_tensor_tensor(
                rstd[par][:], ssq[par][:], 1.0 / 1024.0, tmp4[par][:],
                ALU.mult, ALU.subtract))(),
              deps=[("ssq2", qx, NJ - 1), ("rsq", qx - 2)],
              inc=("dve", 1), label=("agV", qx))
            E("vector", (lambda par=par: lambda e: e.tensor_scalar(
                rstd[par][:], rstd[par][:], EPS, None, ALU.add))(),
              deps=[], inc=("dve", 1), label=("veps", qx))

        qc = i
        if vq(qc):
            for m in range(KD):
                fns = [(lambda qc=qc, m=m, kk=kk: lambda e: e.matmul(
                    psV[m % 4][:], wP1["wbx"][:, kk, m * P:(m + 1) * P],
                    ksl(xg[qc % 2], kk), start=(kk == 0), stop=(kk == KD - 1)))()
                    for kk in range(KD)]
                E("tensor", fns,
                  deps=[("xg", qc, 7), ("vev", qc, m - 4), ("a1", qc - 2, (m % 4) + 4),
                        G1TOT],
                  inc=("pe", 1), label=("V", qc, m))
                E("vector", (lambda qc=qc, m=m: lambda e: e.tensor_scalar(
                    ksl(vb, m), psV[m % 4][:], pk(BBX, m), None, ALU.add))(),
                  deps=[("V", qc, m), ("GA", qc - 1, 7), ("gxv", qc - 1),
                        setup_gsp],
                  inc=("dve", 1), label=("vev", qc, m))

        qh = i - 1
        if vq(qh):
            E("gpsimd", (lambda qh=qh: lambda e: e.tensor_copy(
                hbb[:], thx[:]))(),
              deps=[("scan", qh, 7), ("OB", qh - 1, 7)],
              inc=("gp", 1), label=("hb", qh))
        if vq(qt):
            E("scalar", (lambda qt=qt: lambda e: e.dma_start(
                out=xnb_h[:, cdup(qt), :], in_=xnbb[:]))(),
              deps=[("spxnb", qt - 1)], inc=("sxnb", 16), label=("spxnb", qt))
            E("scalar", (lambda qt=qt: lambda e: e.dma_start(
                out=xg_h[:, cdup(qt), :], in_=xg[qt % 2][:]))(),
              deps=[("xg", qt, 7), ("spxg", qt - 1)],
              inc=("sxg", 16), label=("spxg", qt))
        if vq(qo):
            E("scalar", (lambda qo=qo: lambda e: e.dma_start(
                out=a1_h[:, cdup(qo), :], in_=a1b[:]))(),
              deps=[("a1", qo, 7), ("spA1", qo - 1)],
              inc=("sa1", 16), label=("spA1", qo))

        if vq(qc):
            for m in range(KD):
                fns = [(lambda qc=qc, m=m, kk=kk: lambda e: e.matmul(
                    psG[m % 4][:], wP1["wgx"][:, kk, m * P:(m + 1) * P],
                    ksl(vb, kk), start=(kk == 0), stop=(kk == KD - 1)))()
                    for kk in range(KD)]
                E("tensor", fns,
                  deps=[("vev", qc, 7), ("thx", qc, m - 4), ("Tev3", qc + 1, (m % 4) + 4),
                        G1TOT],
                  inc=("pe", 1), label=("GX", qc, m))
                E("scalar", (lambda qc=qc, m=m: lambda e: e.activation(
                    ksl(thx, m), psG[m % 4][:], AF.Tanh, bias=hbgx_c(m),
                    scale=0.5))(),
                  deps=[("GX", qc, m), ("hb", qc - 1), setup_dve],
                  inc=("act", 1), label=("thx", qc, m))
            for m in range(KD):
                fns = [(lambda qc=qc, m=m, kk=kk: lambda e: e.matmul(
                    psG[m % 4][:], wP1["wga"][:, kk, m * P:(m + 1) * P],
                    ksl(vb, kk), start=(kk == 0), stop=(kk == KD - 1)))()
                    for kk in range(KD)]
                E("tensor", fns,
                  deps=[("thx", qc, (m % 4) + 4) if m < 4 else ("ga1", qc, m - 4),
                        G1TOT],
                  inc=("pe", 1), label=("GA", qc, m))
                E("scalar", (lambda qc=qc, m=m: lambda e: e.activation(
                    sc_a[m % 2][:], psG[m % 4][:], AF.Tanh, bias=hbga_c(m),
                    scale=0.5))(),
                  deps=[("GA", qc, m), ("gt", qc + 1, 7), ("si", qc - 1)],
                  inc=("act", 1), label=("ga1", qc, m))
                E("scalar", (lambda qc=qc, m=m: lambda e: e.activation(
                    ksl(abuf, m), sc_a[m % 2][:], AF.Exp,
                    bias=whalf_c(m), scale=whalf_c(m)))(),
                  deps=[("scan", qc - 1, 7), "su4", setup_dve],
                  inc=("act", 1), label=("aw", qc, m))
            E("vector", (lambda qc=qc: lambda e: e.scalar_tensor_tensor(
                thx[:], thx[:], 1.0, vb[:], ALU.add, ALU.mult))(),
              deps=[("thx", qc, 7), ("vev", qc, 7)],
              inc=("dve", 1), label=("gxv", qc))
            for k in range(KD):
                E("gpsimd", (lambda qc=qc, k=k: lambda e: e.tensor_tensor(
                    ksl(a1b, k), ksl(abuf, k), ksl(abuf, k), ALU.mult))(),
                  deps=[("aw", qc, k), ("spA1", qc - 2), ("a1", qc - 2, 7), setup_pe],
                  inc=("gp", 1), label=("sqa", qc, k))
            E("scalar", (lambda: lambda e: e.activation(
                a1b[:], a1b[:], AF.Sqrt, bias=q25[:, 0:1], scale=-0.25))(),
              deps=[("sqa", qc, 7), "q25"], inc=("act", 1), label=("sc2", qc))
        if vq(qx):
            E("scalar", (lambda qx=qx: lambda e: e.activation(
                tmp4[qx % 2][:], rstd[qx % 2][:], AF.Sqrt))(),
              deps=[("veps", qx)], inc=("act", 1), label=("rsq", qx))
            E("vector", (lambda qx=qx: lambda e: e.reciprocal(
                rstd[qx % 2][:], tmp4[qx % 2][:]))(),
              deps=[("rsq", qx)], inc=("dve", 1), label=("rcp", qx))
            for j in range(NJ):
                E("vector", (lambda qx=qx, j=j: lambda e: e.tensor_scalar(
                    xbuf[qx % 2][:, j, :], xbuf[qx % 2][:, j, :],
                    mvm[qx % 2][:, j:j + 1],
                    rstd[qx % 2][:, j:j + 1], ALU.subtract, ALU.mult))(),
                  deps=[("rcp", qx)], inc=("dve", 1), label=("nrm", qx, j))
            if dbg:
                E("scalar", (lambda qx=qx: lambda e: e.dma_start(
                    out=dbg_h[:, cdup(qx), 48:52], in_=mvm[qx % 2][:, 0:NJ]))(),
                  deps=[("nrm", qx, NJ - 1)], inc=("cp", 16))
                E("scalar", (lambda qx=qx: lambda e: e.dma_start(
                    out=dbg_h[:, cdup(qx), 56:60], in_=rstd[qx % 2][:, 0:NJ]))(),
                  deps=[("nrm", qx, NJ - 1)], inc=("cp", 16))
        if vq(qc):
            E("vector", (lambda: lambda e: e.tensor_tensor(
                thx[:], a1b[:], thx[:], ALU.mult))(),
              deps=[("sc2", qc), ("gxv", qc)], inc=("dve", 1), label=("si", qc))
            for k in range(KD):
                ini = 0.0 if qc == 0 else carry[:, k * 64:k * 64 + 1]
                E("vector", (lambda k=k, ini=ini: lambda e: e.tensor_tensor_scan(
                    ksl(thx, k)[:, ::-1], ksl(abuf, k)[:, ::-1],
                    ksl(thx, k)[:, ::-1], ini, ALU.mult, ALU.add))(),
                  deps=[("si", qc)], inc=("dve", 1), label=("scan", qc, k))
            E("vector", (lambda: lambda e: e.tensor_copy(
                carry[:].rearrange("p (k n) -> p k n", k=KD),
                thx[:].rearrange("p (k n) -> p k n", k=KD)[:, :, 0:64]))(),
              deps=[("scan", qc, 7)], inc=("dve", 1), label=("cv", qc))


    BAR = [("#", "pe", pl.cnt["pe"]), ("#", "act", pl.cnt["act"]),
           ("#", "dve", pl.cnt["dve"]), ("#", "gp", pl.cnt["gp"]),
           ("#", "sxg", pl.cnt["sxg"]),
           ("#", "sxnb", pl.cnt["sxnb"]), ("#", "sa1", pl.cnt["sa1"]),
           ("#", "xld", pl.cnt["xld"])]

    psF = (pb[0], pb[1])
    psU = (pb[2], pb[3])
    psN = (pb[4], pb[5])
    psP = (pb[6], pb[7])

    for nm, src in (("wf1", None), ("wf2", None)):
        for k in range(KD):
            E("gpsimd", (lambda nm=nm, k=k: lambda e: e.dma_start(
                out=wP2[nm][:, k, :], in_=w_r[nm][k]))(),
              deps=BAR, inc=("g2", 16), label=("w2", nm, k))
    for k in range(KD):
        E("gpsimd", (lambda k=k: lambda e: e.dma_start(
            out=wP2["wc"][:, k, :], in_=wc_h[:, k, :]))(),
          deps=BAR + [("#", "gsp", 32)], inc=("g2", 16), label=("w2", "wc", k))
    G2TOT = ("#", "g2", pl.cnt["g2"])
    E("sync", lambda e: e.dma_start(out=afw[:], in_=afwbc_h),
      deps=BAR, inc=("x2", 16), label="afb")

    for i in range(-2, NCH + 1):
        jl = i + 1
        if vq(jl):
            E("sync", (lambda jl=jl: lambda e: e.dma_start(
                out=xgin[jl % 2][:], in_=xg_h[:, jl, :]))(),
              deps=BAR + [("U", jl - 2, 7), ("p2xg", jl - 1)],
              inc=("p2g", 16), label=("p2xg", jl))
            E("sync", (lambda jl=jl: lambda e: e.dma_start(
                out=xnbin[jl % 2][:], in_=xnb_h[:, jl, :]))(),
              deps=BAR + [("FF", jl - 2, 7), ("preA", jl - 2), ("p2xnb", jl - 1)],
              inc=("p2n", 16), label=("p2xnb", jl))

        jf = i - 1
        if vq(jf):
            for m in range(KD):
                fns = [(lambda jf=jf, m=m, kk=kk: lambda e: e.matmul(
                    psF[m % 2][:], wP2["wf2"][:, kk, m * P:(m + 1) * P],
                    ksl(hfb, kk), start=(kk == 0), stop=(kk == KD - 1)))()
                    for kk in range(KD)]
                E("tensor", fns,
                  deps=BAR + [("hf", jf), ("ssum", jf, m - 2),
                              ("ssum", jf - 1, 6 + (m % 2)), G2TOT],
                  inc=("pe", 1), label=("OF", jf, m))
                E("vector", (lambda jf=jf, m=m: lambda e: e.scalar_tensor_tensor(
                    ksl(a1in[jf % 2], m), psF[m % 2][:], pk(BF2, m),
                    ksl(a1in[jf % 2], m), ALU.add, ALU.add))(),
                  deps=[("OF", jf, m), ("preA", jf)],
                  inc=("dve", 1), label=("ssum", jf, m))
                E("gpsimd", (lambda jf=jf, m=m: lambda e: e.tensor_tensor(
                    ksl(a1in[jf % 2], m), ksl(a1in[jf % 2], m),
                    ksl(gbuf, m), ALU.mult))(),
                  deps=[("ssum", jf, m), ("gev", jf, m)],
                  inc=("gp", 1), label=("prod", jf, m))
            for g in range(KD):
                jj, h = divmod(g, 2)
                fns = [(lambda jf=jf, jj=jj, h=h, kk=kk: lambda e: e.transpose(
                    psP[(jj * 2 + h) % 2][:, kk * P:(kk + 1) * P],
                    a1in[jf % 2][:, (h * 4 + kk) * SC + jj * P:
                                 (h * 4 + kk) * SC + (jj + 1) * P],
                    ident[:]))() for kk in list(range(4)) + [3]]
                E("tensor", fns,
                  deps=BAR + [("prod", jf, h * 4 + 3), ("oadd", jf, g - 2),
                              ("oadd", jf - 1, 6 + (g % 2))],
                  inc=("pe", 1), label=("Tp", jf, g))
                E("vector", (lambda jf=jf, jj=jj, h=h, g=g: lambda e:
                             e.tensor_tensor(
                                 xin[jf % 2][:, jj, h * SC:(h + 1) * SC],
                                 xin[jf % 2][:, jj, h * SC:(h + 1) * SC],
                                 psP[g % 2][:], ALU.add))(),
                  deps=[("Tp", jf, g), ("p2x", jf)],
                  inc=("dve", 1), label=("oadd", jf, g))
            E("scalar", (lambda jf=jf: lambda e: e.dma_start(
                out=y_r[jf], in_=xin[jf % 2][:]))(),
              deps=[("oadd", jf, 7), ("yst", jf - 1)],
              inc=("yst", 16), label=("yst", jf))

        jc = i
        if vq(jc):
            for m in range(KD):
                fns = [(lambda jc=jc, m=m, kk=kk: lambda e: e.matmul(
                    psU[m % 2][:], wP2["wf1"][:, kk, m * P:(m + 1) * P],
                    ksl(xgin[jc % 2], kk), start=(kk == 0), stop=(kk == KD - 1)))()
                    for kk in range(KD)]
                E("tensor", fns,
                  deps=BAR + [("p2xg", jc), ("uev", jc, m - 2),
                              ("uev", jc - 1, 6 + (m % 2)), G2TOT],
                  inc=("pe", 1), label=("U", jc, m))
                E("scalar", (lambda jc=jc, m=m: lambda e: e.activation(
                    ksl(ubuf, m), psU[m % 2][:], AF.Identity, bias=pk(BF1, m)))(),
                  deps=[("U", jc, m), ("p2cv", jc - 1), ("hf", jc - 1)],
                  inc=("act", 1), label=("uev", jc, m))
            for m in range(KD):
                fns = [(lambda jc=jc, m=m, kk=kk: lambda e: e.matmul(
                    psN[m % 2][:], wP2["wc"][:, kk, m * P:(m + 1) * P],
                    ksl(xnbin[jc % 2], kk), start=(kk == 0), stop=(kk == KD - 1)))()
                    for kk in range(KD)]
                E("tensor", fns,
                  deps=BAR + [("p2xnb", jc), ("gev", jc, m - 2),
                              ("gev", jc - 1, 6 + (m % 2)), G2TOT],
                  inc=("pe", 1), label=("FF", jc, m))
                E("scalar", (lambda jc=jc, m=m: lambda e: e.activation(
                    ksl(gbuf, m), psN[m % 2][:], AF.Silu, bias=pk(BC, m)))(),
                  deps=[("FF", jc, m), ("prod", jc - 1, m)],
                  inc=("act", 1), label=("gev", jc, m))
            E("gpsimd", (lambda jc=jc: lambda e: e.tensor_tensor(
                a1in[jc % 2][:], a1in[jc % 2][:], xnbin[jc % 2][:], ALU.add))(),
              deps=[("p2a1", jc), ("p2xnb", jc)],
              inc=("gp", 1), label=("preA", jc))
            for k in range(KD):
                ini = 0.0 if jc == 0 else carry[:, k * 64 + 63:k * 64 + 64]
                E("vector", (lambda k=k, ini=ini, jc=jc: lambda e:
                             e.tensor_tensor_scan(
                                 ksl(ubuf, k), ksl(afw, k), ksl(ubuf, k),
                                 ini, ALU.mult, ALU.add))(),
                  deps=[("uev", jc, 7), "afb"],
                  inc=("dve", 1), label=("p2scan", jc, k))
            E("vector", (lambda: lambda e: e.tensor_copy(
                carry[:].rearrange("p (k n) -> p k n", k=KD),
                ubuf[:].rearrange("p (k n) -> p k n", k=KD)[:, :, SC - 64:SC]))(),
              deps=[("p2scan", jc, 7)], inc=("dve", 1), label=("p2cv", jc))
            E("scalar", (lambda: lambda e: e.activation(
                hfb[:], ubuf[:], AF.Identity))(),
              deps=[("p2scan", jc, 7), ("OF", jc - 1, 7)],
              inc=("act", 1), label=("hf", jc))
        if vq(jl):
            E("sync", (lambda jl=jl: lambda e: e.dma_start(
                out=xin[jl % 2][:], in_=x_r[jl]))(),
              deps=BAR + [("yst", jl - 2), ("p2x", jl - 1), "afb"],
              inc=("x2", 16), label=("p2x", jl))
            E("sync", (lambda jl=jl: lambda e: e.dma_start(
                out=a1in[jl % 2][:], in_=a1_h[:, jl, :]))(),
              deps=BAR + [("Tp", jl - 2, 7), ("p2a1", jl - 1)],
              inc=("p2a", 16), label=("p2a1", jl))

    fwd = [d for d in pl.skipped if d in pl.ev]
    assert not fwd, f"forward-referenced deps (ordering bugs): {sorted(set(fwd))[:20]}"

    E("sync", lambda e: e.nop(), deps=[("#", "yst", pl.cnt["yst"])], inc=None)

    es = contextlib.ExitStack()
    with es:
        sems = {s: es.enter_context(nc.semaphore(f"sem_{s}")) for s in SEMS}
        block = es.enter_context(nc.Block())

        @block.sync
        def _(eng):
            pl.run("sync", eng, sems)

        @block.scalar
        def _(eng):
            pl.run("scalar", eng, sems)

        @block.vector
        def _(eng):
            pl.run("vector", eng, sems)

        @block.tensor
        def _(eng):
            pl.run("tensor", eng, sems)

        @block.gpsimd
        def _(eng):
            pl.run("gpsimd", eng, sems)

    return nc


def kernel(**inputs):
    x = np.asarray(inputs["L0_attn_x"], dtype=np.float32)
    vecs = [np.asarray(inputs[k], dtype=np.float32) for k in (
        "L0_attn_bf1", "L0_attn_bf2", "L0_attn_bbx", "L0_attn_bgx",
        "L0_attn_bga", "L0_attn_bb2", "L0_ffn_b", "L0_attn_a_fwd",
        "L0_attn_a_param")]
    packs = np.concatenate([v.reshape(KD, P).T for v in vecs], axis=1)
    packs = np.ascontiguousarray(packs, dtype=np.float32)
    base = {
        "wf1": np.ascontiguousarray(inputs["L0_attn_Wf1"], np.float32),
        "wf2": np.ascontiguousarray(inputs["L0_attn_Wf2"], np.float32),
        "wbx": np.ascontiguousarray(inputs["L0_attn_Wbx"], np.float32),
        "wgx": np.ascontiguousarray(inputs["L0_attn_Wgx"], np.float32),
        "wga": np.ascontiguousarray(inputs["L0_attn_Wga"], np.float32),
        "wb2": np.ascontiguousarray(inputs["L0_attn_Wb2"], np.float32),
        "wT": np.ascontiguousarray(np.asarray(inputs["L0_ffn_W"]).T, np.float32),
        "dna": np.ascontiguousarray(inputs["L0_ffn_down_aux"], np.float32),
        "uaT": np.ascontiguousarray(np.asarray(inputs["L0_ffn_up_aux"]).T,
                                    np.float32),
        "ldT": np.ascontiguousarray(np.asarray(inputs["L0_ffn_lora_down"]).T,
                                    np.float32),
        "luT": np.ascontiguousarray(np.asarray(inputs["L0_ffn_lora_up"]).T,
                                    np.float32),
        "packs": packs,
        "ident": np.eye(P, dtype=np.float32),
        "afwbc": np.ascontiguousarray(np.repeat(
            np.asarray(inputs["L0_attn_a_fwd"], np.float32).reshape(KD, P).T
            [:, :, None], SC, axis=2).reshape(P, KD * SC)),
    }
    nc = build_nc()
    in_maps = []
    for b in range(8):
        m = dict(base)
        m["x"] = np.ascontiguousarray(x[b], np.float32)
        in_maps.append(m)
    res = run_bass_kernel_spmd(nc, in_maps, core_ids=list(range(8)))
    return np.stack([r["y"] for r in res.results], axis=0)


# version 08
# speedup vs baseline: 1.0479x; 1.0479x over previous
import contextlib
import numpy as np

import concourse.bass as bass
import concourse.mybir as mybir
from concourse.bass_utils import run_bass_kernel_spmd

F32 = mybir.dt.float32
BF16 = mybir.dt.bfloat16
AF = mybir.ActivationFunctionType
ALU = mybir.AluOpType

P = 128
D = 1024
KD = 8
S = 4096
SC = 512
NCH = S // SC
NJ = SC // P

C1 = 0.7978845608028654
C3 = C1 * 0.044715
EPS = 1e-5

SEMS = ("cp", "cwT", "g1", "g2", "gsp", "xld", "x2", "p2g", "p2n", "p2a",
        "sxg", "sxnb", "sa1", "yst", "pe", "act", "dve", "gp")
OWN = {"scalar": "act", "vector": "dve", "tensor": "pe", "gpsimd": "gp"}


class Plan:
    def __init__(self):
        self.q = {n: [] for n in ("sync", "scalar", "vector", "tensor", "gpsimd")}
        self.cnt = {s: 0 for s in SEMS}
        self.ev = {}
        self.skipped = []

    def emit(self, qn, fns, deps=(), inc=None, label=None):
        if not isinstance(fns, (list, tuple)):
            fns = [fns]
        ws = {}
        for d in deps:
            if d is None:
                continue
            if isinstance(d, tuple) and len(d) == 3 and d[0] == "#":
                s, c = d[1], d[2]
            else:
                if d not in self.ev:
                    self.skipped.append(d)
                    continue
                s, c = self.ev[d]
            if c > ws.get(s, 0):
                ws[s] = c
        if inc is not None:
            s, n = inc
            self.cnt[s] += n
            if label is not None:
                self.ev[label] = (s, self.cnt[s])
        self.q[qn].append((list(fns), ws, inc, label))

    def run(self, qn, eng, sems):
        hw = {}
        own = OWN.get(qn)
        for fns, ws, inc, _lb in self.q[qn]:
            for s in SEMS:
                c = ws.get(s)
                if c is None or s == own or hw.get(s, 0) >= c:
                    continue
                eng.wait_ge(sems[s], c)
                hw[s] = c
            inst = None
            for f in fns:
                inst = f(eng)
            if inc is not None and inst is not None:
                inst.then_inc(sems[inc[0]], inc[1])


def build_nc(detect_races=True, dbg=False):
    nc = bass.Bass("TRN2", target_bir_lowering=False,
                   detect_race_conditions=detect_races)

    x_h = nc.dram_tensor("x", [S, D], F32, kind="ExternalInput").ap()
    w_h = {}
    for nm in ("wf1", "wf2", "wbx", "wgx", "wga", "wb2", "wT"):
        w_h[nm] = nc.dram_tensor(nm, [D, D], F32, kind="ExternalInput").ap()
    dna_h = nc.dram_tensor("dna", [100, D], F32, kind="ExternalInput").ap()
    uaT_h = nc.dram_tensor("uaT", [50, D], F32, kind="ExternalInput").ap()
    ldT_h = nc.dram_tensor("ldT", [100, 4], F32, kind="ExternalInput").ap()
    luT_h = nc.dram_tensor("luT", [4, 50], F32, kind="ExternalInput").ap()
    packs_h = nc.dram_tensor("packs", [P, 72], F32, kind="ExternalInput").ap()
    afwbc_h = nc.dram_tensor("afwbc", [P, KD * SC], F32, kind="ExternalInput").ap()
    ident_h = nc.dram_tensor("ident", [P, P], F32, kind="ExternalInput").ap()
    y_h = nc.dram_tensor("y", [S, D], F32, kind="ExternalOutput").ap()
    KND = "ExternalOutput" if dbg else "Internal"
    a1_h = nc.dram_tensor("a1s", [P, NCH, KD * SC], F32, kind=KND).ap()
    xg_h = nc.dram_tensor("xgs", [P, NCH, KD * SC], BF16, kind=KND).ap()
    xnb_h = nc.dram_tensor("xnbs", [P, NCH, KD * SC], BF16, kind=KND).ap()
    wc_h = nc.dram_tensor("wcs", [P, KD, D], BF16, kind=KND).ap()
    if dbg:
        dbg_h = nc.dram_tensor("dbgs", [P, NCH, 64], F32, kind="ExternalOutput").ap()

    x_r = x_h.rearrange("(c j p) d -> c p j d", p=P, j=NJ)
    y_r = y_h.rearrange("(c j p) d -> c p j d", p=P, j=NJ)
    w_r = {nm: w_h[nm].rearrange("(k p) n -> k p n", p=P) for nm in w_h}

    align = lambda o: (o + 31) & ~31
    base = align(nc.sbuf_base)
    off = base

    def at(name, shape, dtype, o):
        return nc.alloc_sbuf_tensor_at(name, shape, dtype, offset=o)

    def persist(name, shape, dtype):
        nonlocal off
        t = at(name, shape, dtype, off)
        nbytes = int(np.prod(shape[1:])) * (4 if dtype == F32 else 2)
        off = align(off + nbytes)
        return t

    ident = persist("identt", [P, P], F32)
    q25 = persist("q25c", [P, 1], F32)
    packs = persist("packs_s", [P, 72], F32)
    dhalf = persist("dhalf", [P, 72], F32)
    dwh = persist("dwh", [P, 72], F32)
    carry = persist("carryb", [P, KD * 64], F32)
    mvm = [persist(f"mvm{i}", [P, 64], F32) for i in range(2)]
    ssq = [persist(f"ssq{i}", [P, 64], F32) for i in range(2)]
    smv = [persist(f"smv{i}", [P, 64], F32) for i in range(2)]
    tmp4 = [persist(f"tmp4{i}", [P, 64], F32) for i in range(2)]
    rstd = [persist(f"rstd{i}", [P, 64], F32) for i in range(2)]

    region = align(off)
    TOP = nc.sbuf_top

    o = region
    def ph(name, shape, dtype):
        nonlocal o
        t = at(name, shape, dtype, o)
        nbytes = int(np.prod(shape[1:])) * (4 if dtype == F32 else 2)
        o = align(o + nbytes)
        return t

    xbuf = [ph(f"xb{i}", [P, NJ, D], F32) for i in range(2)]
    sc_xn = [ph(f"scxn{i}", [P, SC], F32) for i in range(2)]
    sc_a = [ph(f"sca{i}", [P, SC], F32) for i in range(2)]
    sc_b = [ph(f"scb{i}", [P, SC], F32) for i in range(2)]
    xg = [ph(f"xg{i}", [P, KD * SC], BF16) for i in range(2)]
    xnbb = ph("xnbb", [P, KD * SC], BF16)
    vb = ph("vb", [P, KD * SC], BF16)
    thx = ph("thx", [P, KD * SC], F32)
    abuf = ph("abuf", [P, KD * SC], F32)
    hbb = ph("hbb", [P, KD * SC], BF16)
    a1b = ph("a1b", [P, KD * SC], F32)
    wP1 = {nm: ph(f"w_{nm}", [P, KD, D], BF16)
           for nm in ("wbx", "wgx", "wga", "wb2")}
    junk = ph("junk", [P, D], F32)
    assert o <= TOP, f"phase1 sbuf overflow {o} > {TOP}"

    so = region
    so = None
    a1b_off = None
    a1b_off = nc.lookup_mloc(a1b).addr
    so = a1b_off
    def su(name, shape, dtype):
        nonlocal so
        t = at(name, shape, dtype, so)
        nbytes = int(np.prod(shape[1:])) * (4 if dtype == F32 else 2)
        so = align(so + nbytes)
        return t
    t1b = su("t1b", [4, D], BF16)
    t2b = su("t2b", [50, D], BF16)
    ldTb = su("ldTb", [100, 4], BF16)
    luTb = su("luTb", [4, 50], BF16)
    dnab = su("dnab", [100, D], BF16)
    uaTb = su("uaTb", [50, D], BF16)
    assert so <= a1b_off + KD * SC * 4

    o2 = region
    def p2(name, shape, dtype):
        nonlocal o2
        t = at(name, shape, dtype, o2)
        nbytes = int(np.prod(shape[1:])) * (4 if dtype == F32 else 2)
        o2 = align(o2 + nbytes)
        return t

    xgin = [p2(f"xgin{i}", [P, KD * SC], BF16) for i in range(2)]
    xnbin = [p2(f"xnbin{i}", [P, KD * SC], BF16) for i in range(2)]
    a1in = [p2(f"a1in{i}", [P, KD * SC], F32) for i in range(2)]
    xin = [p2(f"xin{i}", [P, NJ, D], F32) for i in range(2)]
    ubuf = p2("ubuf", [P, KD * SC], F32)
    afw = p2("afw", [P, KD * SC], F32)
    hfb = p2("hfb", [P, KD * SC], BF16)
    gbuf = p2("gbuf", [P, KD * SC], F32)
    wP2 = {nm: p2(f"w2_{nm}", [P, KD, D], BF16) for nm in ("wf1", "wf2", "wc")}
    assert o2 <= TOP, f"phase2 sbuf overflow {o2} > {TOP}"

    pb = [nc.alloc_psum_tensor(f"pb{i}", [P, SC], F32) for i in range(8)]

    pl = Plan()
    E = pl.emit
    ev = pl.ev

    def pk(v, k):
        return packs[:, v * KD + k:v * KD + k + 1]
    BF1, BF2, BBX, BGX, BGA, BB2, BC, AFW, APR = range(9)
    def hbgx_c(k):
        return dhalf[:, BGX * KD + k:BGX * KD + k + 1]
    def hbga_c(k):
        return dhalf[:, BGA * KD + k:BGA * KD + k + 1]
    def whalf_c(k):
        return dwh[:, APR * KD + k:APR * KD + k + 1]

    vq = lambda q: 0 <= q < NCH

    E("sync", lambda e: e.dma_start(out=packs[:], in_=packs_h),
      inc=("cp", 16))
    E("sync", lambda e: e.dma_start(out=ident[:], in_=ident_h),
      inc=("cp", 16), label="Lcp")
    for nm in ("wbx", "wgx", "wga", "wb2"):
        for k in range(KD):
            E("gpsimd",
              (lambda nm=nm, k=k: lambda e: e.dma_start(
                  out=wP1[nm][:, k, :], in_=w_r[nm][k]))(),
              inc=("g1", 16), label=("wld", nm, k))
    for nm, dst, src in (("ldT", ldTb, ldT_h), ("luT", luTb, luT_h),
                         ("dna", dnab, dna_h), ("uaT", uaTb, uaT_h)):
        E("gpsimd", (lambda dst=dst, src=src: lambda e: e.dma_start(
            out=dst[:], in_=src))(), inc=("g1", 16), label=("wld", nm))
    for k in range(KD):
        dst = thx if k < 4 else abuf
        kk = k % 4
        E("sync", (lambda dst=dst, kk=kk, k=k: lambda e: e.dma_start(
            out=dst[:, kk * D:(kk + 1) * D], in_=w_r["wT"][k]))(),
          inc=("cwT", 16), label=("wTld", k))

    G1TOT = ("#", "g1", pl.cnt["g1"])
    CWTOT = ("#", "cwT", pl.cnt["cwT"])
    E("scalar", lambda e: e.activation(dwh[:], packs[:], AF.Exp),
      deps=["Lcp"], inc=("act", 1), label="su1")
    E("vector", lambda e: e.tensor_scalar(dwh[:], dwh[:], 1.0, None, ALU.add),
      deps=["su1"], inc=("dve", 1), label="su2")
    E("scalar", lambda e: e.activation(dwh[:], dwh[:], AF.Ln),
      deps=["su2"], inc=("act", 1), label="su3")
    E("vector", lambda e: e.tensor_scalar(dwh[:], dwh[:], -4.0, None, ALU.mult),
      deps=["su3"], inc=("dve", 1), label="su4")
    E("vector", lambda e: e.tensor_scalar(dhalf[:], packs[:], 0.5, None, ALU.mult),
      deps=["Lcp"], inc=("dve", 1), label="su5")
    E("vector", lambda e: e.memset(q25[:], 0.25),
      deps=[], inc=("dve", 1), label="q25")
    for bi, buf in enumerate((smv[0], smv[1], ssq[0], ssq[1])):
        E("vector", (lambda buf=buf: lambda e: e.tensor_scalar(
            buf[:], ident[:, 0:64], 0.0, None, ALU.mult))(),
          deps=["Lcp"], inc=("dve", 1), label=("szero", bi))

    for h in range(2):
        E("tensor", (lambda h=h: lambda e: e.matmul(
            pb[6 + h][0:4, :], ldTb[:], dnab[:, h * SC:(h + 1) * SC],
            start=True, stop=True))(),
          deps=[G1TOT], inc=("pe", 1),
          label=("suT1", h))
        E("vector", (lambda h=h: lambda e: e.tensor_copy(
            t1b[:, h * SC:(h + 1) * SC], pb[6 + h][0:4, :]))(),
          deps=[("suT1", h)], inc=("dve", 1), label=("suT1e", h))
    for h in range(2):
        E("tensor", (lambda h=h: lambda e: e.matmul(
            pb[6 + h][0:50, :], luTb[:], t1b[:, h * SC:(h + 1) * SC],
            start=True, stop=True))(),
          deps=[G1TOT, ("suT1e", 0), ("suT1e", 1)], inc=("pe", 1),
          label=("suT2", h))
        E("vector", (lambda h=h: lambda e: e.tensor_copy(
            t2b[:, h * SC:(h + 1) * SC], pb[6 + h][0:50, :]))(),
          deps=[("suT2", h)], inc=("dve", 1), label=("suT2e", h))
    for half in range(2):
        for it4 in range(4):
            it = half * 4 + it4
            wtsrc = thx if it < 4 else abuf
            wslice = (it % 4)
            for h in range(2):
                E("tensor", (lambda it=it, h=h: lambda e: e.matmul(
                    pb[6 + h][:], t2b[:, it * P:(it + 1) * P],
                    uaTb[:, h * SC:(h + 1) * SC], start=True, stop=True))(),
                  deps=[G1TOT, CWTOT, ("suT2e", 0), ("suT2e", 1),
                        ("suWc", half, it4 - 1, h) if it4 > 0
                        else ("suWc", half - 1, 3, h)],
                  inc=("pe", 1), label=("suL", it, h))
                E("vector", (lambda it=it, h=h, wtsrc=wtsrc, ws=wslice,
                             half=half, it4=it4: lambda e:
                             e.scalar_tensor_tensor(
                                 vb[:].rearrange("p (a n) -> p a n", a=8)
                                 [:, it4 * 2 + h, :],
                                 pb[6 + h][:], 1.0,
                                 wtsrc[:, ws * D + h * SC: ws * D + (h + 1) * SC],
                                 ALU.mult, ALU.add))(),
                  deps=[("suL", it, h)] +
                       ([("#", "gsp", 16)] if half == 1 else []),
                  inc=("dve", 1),
                  label=("suWc", half, it4, h))
        E("gpsimd", (lambda half=half: lambda e: e.dma_start(
            out=wc_h[:, half * 4:(half + 1) * 4, :].rearrange("p a n -> p (a n)"),
            in_=vb[:]))(),
          deps=[("suWc", half, 3, 1), ("suSp", half - 1)],
          inc=("gsp", 16), label=("suSp", half))
    setup_dve = ("#", "dve", pl.cnt["dve"])
    setup_gsp = ("#", "gsp", pl.cnt["gsp"])
    setup_pe = ("#", "pe", pl.cnt["pe"])

    psT = (pb[0], pb[1], pb[2], pb[3])
    psG = (pb[0], pb[1], pb[2], pb[3])
    psV = (pb[4], pb[5], pb[6], pb[7])
    psB = (pb[4], pb[5], pb[6], pb[7])

    def cdup(q):
        return NCH - 1 - q

    def ksl(buf, k):
        return buf[:, k * SC:(k + 1) * SC]

    for i in range(-2, NCH + 2):
        qx = i + 2
        if vq(qx):
            E("sync", (lambda qx=qx: lambda e: e.dma_start(
                out=xbuf[qx % 2][:], in_=x_r[cdup(qx)]))(),
              deps=[("T", qx - 2, 7), ("xld", qx - 1)],
              inc=("xld", 16), label=("xld", qx))

        qt = i + 1
        qo = i - 2
        for k in range(KD):
            if vq(qt):
                fns = [(lambda qt=qt, k=k, j=j: lambda e: e.transpose(
                    psT[k % 4][:, j * P:(j + 1) * P],
                    xbuf[qt % 2][:, j, k * P:(k + 1) * P], ident[:]))()
                    for j in list(range(NJ)) + [NJ - 1]]
                E("tensor", fns,
                  deps=[("nrm", qt, NJ - 1), ("Tev3", qt, k - 4),
                        ("ga1", qt - 2, (k % 4) + 4), "Lcp"],
                  inc=("pe", 1), label=("T", qt, k))
            if vq(qo):
                fns = [(lambda qo=qo, k=k, kk=kk: lambda e: e.matmul(
                    psB[k % 4][:], wP1["wb2"][:, kk, k * P:(k + 1) * P],
                    ksl(hbb, kk), start=(kk == 0), stop=(kk == KD - 1)))()
                    for kk in range(KD)]
                E("tensor", fns,
                  deps=[("hb", qo), ("a1", qo, k - 4), ("vev", qo + 1, (k % 4) + 4),
                        G1TOT, setup_dve],
                  inc=("pe", 1), label=("OB", qo, k))
                E("scalar", (lambda qo=qo, k=k: lambda e: e.activation(
                    ksl(a1b, k), psB[k % 4][:], AF.Identity,
                    bias=pk(BB2, k)))(),
                  deps=[("OB", qo, k), ("si", qo + 1), ("spA1", qo - 1)],
                  inc=("act", 1), label=("a1", qo, k))
            if vq(qt):
                E("scalar", (lambda qt=qt, k=k: lambda e: e.activation(
                    sc_xn[k % 2][:], psT[k % 4][:], AF.Identity, scale=0.5))(),
                  deps=[("T", qt, k), ("gt", qt, k - 2), ("xg", qt, k - 2)],
                  inc=("act", 1), label=("Tev2", qt, k))
                E("scalar", (lambda qt=qt, k=k: lambda e: e.activation(
                    ksl(xnbb, k), psT[k % 4][:], AF.Identity))(),
                  deps=[("spxnb", qt - 1)],
                  inc=("act", 1), label=("Tev3", qt, k))
                E("gpsimd", (lambda k=k: lambda e: e.tensor_tensor(
                    sc_a[k % 2][:], sc_xn[k % 2][:], sc_xn[k % 2][:],
                    ALU.mult))(),
                  deps=[("Tev2", qt, k), ("th", qt, k - 2),
                        ("aw", qt - 2, 6 + (k % 2))],
                  inc=("gp", 1), label=("gsq", qt, k))
                E("gpsimd", (lambda k=k: lambda e: e.tensor_scalar(
                    sc_a[k % 2][:], sc_a[k % 2][:], 8.0 * C3, 2.0 * C1,
                    ALU.mult, ALU.add))(),
                  deps=[], inc=("gp", 1), label=("gq", qt, k))
                E("gpsimd", (lambda k=k: lambda e: e.tensor_tensor(
                    sc_a[k % 2][:], sc_a[k % 2][:], sc_xn[k % 2][:],
                    ALU.mult))(),
                  deps=[], inc=("gp", 1), label=("gt", qt, k))
                E("scalar", (lambda k=k: lambda e: e.activation(
                    sc_b[k % 2][:], sc_a[k % 2][:], AF.Tanh))(),
                  deps=[("gt", qt, k), ("xg", qt, k - 2)],
                  inc=("act", 1), label=("th", qt, k))
                E("vector", (lambda qt=qt, k=k: lambda e: e.scalar_tensor_tensor(
                    ksl(xg[qt % 2], k), sc_b[k % 2][:], 1.0, sc_xn[k % 2][:],
                    ALU.add, ALU.mult))(),
                  deps=[("th", qt, k), ("V", qt - 2, 7), ("spxg", qt - 2)],
                  inc=("dve", 1), label=("xg", qt, k))

        if vq(qx):
            par = qx % 2
            for j in range(NJ):
                E("gpsimd", (lambda qx=qx, j=j: lambda e: e.tensor_tensor(
                    junk[:], xbuf[qx % 2][:, j, :], xbuf[qx % 2][:, j, :],
                    ALU.mult))(),
                  deps=[("xld", qx), ("ssq2", qx, j - 1), ("ssq2", qx - 2, NJ - 1),
                        ("agV", qx - 2)],
                  inc=("gp", 1), label=("sqa2", qx, j))
                E("vector", (lambda qx=qx, j=j, par=par: lambda e: e.tensor_reduce(
                    smv[par][:, j:j + 1], xbuf[qx % 2][:, j, :],
                    mybir.AxisListType.X, ALU.add))(),
                  deps=[("xld", qx)], inc=("dve", 1), label=("smv2", qx, j))
                E("vector", (lambda qx=qx, j=j, par=par: lambda e: e.tensor_reduce(
                    ssq[par][:, j:j + 1], junk[:],
                    mybir.AxisListType.X, ALU.add))(),
                  deps=[("sqa2", qx, j)], inc=("dve", 1), label=("ssq2", qx, j))
            E("vector", (lambda par=par: lambda e: e.tensor_scalar(
                mvm[par][:], smv[par][:], 1.0 / 1024.0, None, ALU.mult))(),
              deps=[], inc=("dve", 1), label=("agM", qx))
            E("vector", (lambda par=par: lambda e: e.tensor_tensor(
                tmp4[par][:], mvm[par][:], mvm[par][:], ALU.mult))(),
              deps=[("nrm", qx - 2, NJ - 1)], inc=("dve", 1), label=("agE", qx))
            E("vector", (lambda par=par, qx=qx: lambda e: e.scalar_tensor_tensor(
                rstd[par][:], ssq[par][:], 1.0 / 1024.0, tmp4[par][:],
                ALU.mult, ALU.subtract))(),
              deps=[("ssq2", qx, NJ - 1), ("rsq", qx - 2)],
              inc=("dve", 1), label=("agV", qx))
            E("vector", (lambda par=par: lambda e: e.tensor_scalar(
                rstd[par][:], rstd[par][:], EPS, None, ALU.add))(),
              deps=[], inc=("dve", 1), label=("veps", qx))

        qc = i
        if vq(qc):
            for m in range(KD):
                fns = [(lambda qc=qc, m=m, kk=kk: lambda e: e.matmul(
                    psV[m % 4][:], wP1["wbx"][:, kk, m * P:(m + 1) * P],
                    ksl(xg[qc % 2], kk), start=(kk == 0), stop=(kk == KD - 1)))()
                    for kk in range(KD)]
                E("tensor", fns,
                  deps=[("xg", qc, 7), ("vev", qc, m - 4), ("a1", qc - 2, (m % 4) + 4),
                        G1TOT],
                  inc=("pe", 1), label=("V", qc, m))
                E("vector", (lambda qc=qc, m=m: lambda e: e.tensor_scalar(
                    ksl(vb, m), psV[m % 4][:], pk(BBX, m), None, ALU.add))(),
                  deps=[("V", qc, m), ("GA", qc - 1, 7), ("gxv", qc - 1),
                        setup_gsp],
                  inc=("dve", 1), label=("vev", qc, m))

        qh = i - 1
        if vq(qh):
            E("scalar", (lambda qh=qh: lambda e: e.activation(
                hbb[:], thx[:], AF.Identity))(),
              deps=[("scan", qh, 7), ("OB", qh - 1, 7)],
              inc=("act", 1), label=("hb", qh))
        if vq(qt):
            E("scalar", (lambda qt=qt: lambda e: e.dma_start(
                out=xnb_h[:, cdup(qt), :], in_=xnbb[:]))(),
              deps=[("spxnb", qt - 1)], inc=("sxnb", 16), label=("spxnb", qt))
            E("scalar", (lambda qt=qt: lambda e: e.dma_start(
                out=xg_h[:, cdup(qt), :], in_=xg[qt % 2][:]))(),
              deps=[("xg", qt, 7), ("spxg", qt - 1)],
              inc=("sxg", 16), label=("spxg", qt))
        if vq(qo):
            E("scalar", (lambda qo=qo: lambda e: e.dma_start(
                out=a1_h[:, cdup(qo), :], in_=a1b[:]))(),
              deps=[("a1", qo, 7), ("spA1", qo - 1)],
              inc=("sa1", 16), label=("spA1", qo))

        if vq(qc):
            for m in range(KD):
                fns = [(lambda qc=qc, m=m, kk=kk: lambda e: e.matmul(
                    psG[m % 4][:], wP1["wgx"][:, kk, m * P:(m + 1) * P],
                    ksl(vb, kk), start=(kk == 0), stop=(kk == KD - 1)))()
                    for kk in range(KD)]
                E("tensor", fns,
                  deps=[("vev", qc, 7), ("thx", qc, m - 4), ("Tev3", qc + 1, (m % 4) + 4),
                        G1TOT],
                  inc=("pe", 1), label=("GX", qc, m))
                E("scalar", (lambda qc=qc, m=m: lambda e: e.activation(
                    ksl(thx, m), psG[m % 4][:], AF.Tanh, bias=hbgx_c(m),
                    scale=0.5))(),
                  deps=[("GX", qc, m), ("hb", qc - 1), setup_dve],
                  inc=("act", 1), label=("thx", qc, m))
            for m in range(KD):
                fns = [(lambda qc=qc, m=m, kk=kk: lambda e: e.matmul(
                    psG[m % 4][:], wP1["wga"][:, kk, m * P:(m + 1) * P],
                    ksl(vb, kk), start=(kk == 0), stop=(kk == KD - 1)))()
                    for kk in range(KD)]
                E("tensor", fns,
                  deps=[("thx", qc, (m % 4) + 4) if m < 4 else ("ga1", qc, m - 4),
                        G1TOT],
                  inc=("pe", 1), label=("GA", qc, m))
                E("scalar", (lambda qc=qc, m=m: lambda e: e.activation(
                    sc_a[m % 2][:], psG[m % 4][:], AF.Tanh, bias=hbga_c(m),
                    scale=0.5))(),
                  deps=[("GA", qc, m), ("gt", qc + 1, 7), ("si", qc - 1)],
                  inc=("act", 1), label=("ga1", qc, m))
                E("scalar", (lambda qc=qc, m=m: lambda e: e.activation(
                    ksl(abuf, m), sc_a[m % 2][:], AF.Exp,
                    bias=whalf_c(m), scale=whalf_c(m)))(),
                  deps=[("scan", qc - 1, 7), "su4", setup_dve],
                  inc=("act", 1), label=("aw", qc, m))
            E("vector", (lambda qc=qc: lambda e: e.scalar_tensor_tensor(
                thx[:], thx[:], 1.0, vb[:], ALU.add, ALU.mult))(),
              deps=[("thx", qc, 7), ("vev", qc, 7)],
              inc=("dve", 1), label=("gxv", qc))
            for k in range(KD):
                E("gpsimd", (lambda qc=qc, k=k: lambda e: e.tensor_tensor(
                    ksl(a1b, k), ksl(abuf, k), ksl(abuf, k), ALU.mult))(),
                  deps=[("aw", qc, k), ("spA1", qc - 2), ("a1", qc - 2, 7), setup_pe],
                  inc=("gp", 1), label=("sqa", qc, k))
            E("scalar", (lambda: lambda e: e.activation(
                a1b[:], a1b[:], AF.Sqrt, bias=q25[:, 0:1], scale=-0.25))(),
              deps=[("sqa", qc, 7), "q25"], inc=("act", 1), label=("sc2", qc))
        if vq(qx):
            E("scalar", (lambda qx=qx: lambda e: e.activation(
                tmp4[qx % 2][:], rstd[qx % 2][:], AF.Sqrt))(),
              deps=[("veps", qx)], inc=("act", 1), label=("rsq", qx))
            E("vector", (lambda qx=qx: lambda e: e.reciprocal(
                rstd[qx % 2][:], tmp4[qx % 2][:]))(),
              deps=[("rsq", qx)], inc=("dve", 1), label=("rcp", qx))
            E("vector", (lambda qx=qx: lambda e: e.scalar_tensor_tensor(
                tmp4[qx % 2][:], mvm[qx % 2][:], -1.0, rstd[qx % 2][:],
                ALU.mult, ALU.mult))(),
              deps=[("rcp", qx)], inc=("dve", 1), label=("ngm", qx))
            for j in range(NJ):
                E("scalar", (lambda qx=qx, j=j: lambda e: e.activation(
                    xbuf[qx % 2][:, j, :], xbuf[qx % 2][:, j, :], AF.Identity,
                    bias=tmp4[qx % 2][:, j:j + 1],
                    scale=rstd[qx % 2][:, j:j + 1]))(),
                  deps=[("ngm", qx)], inc=("act", 1), label=("nrm", qx, j))
            if dbg:
                E("scalar", (lambda qx=qx: lambda e: e.dma_start(
                    out=dbg_h[:, cdup(qx), 48:52], in_=mvm[qx % 2][:, 0:NJ]))(),
                  deps=[("nrm", qx, NJ - 1)], inc=("cp", 16))
                E("scalar", (lambda qx=qx: lambda e: e.dma_start(
                    out=dbg_h[:, cdup(qx), 56:60], in_=rstd[qx % 2][:, 0:NJ]))(),
                  deps=[("nrm", qx, NJ - 1)], inc=("cp", 16))
        if vq(qc):
            E("vector", (lambda: lambda e: e.tensor_tensor(
                thx[:], a1b[:], thx[:], ALU.mult))(),
              deps=[("sc2", qc), ("gxv", qc)], inc=("dve", 1), label=("si", qc))
            for k in range(KD):
                ini = 0.0 if qc == 0 else carry[:, k * 64:k * 64 + 1]
                E("vector", (lambda k=k, ini=ini: lambda e: e.tensor_tensor_scan(
                    ksl(thx, k)[:, ::-1], ksl(abuf, k)[:, ::-1],
                    ksl(thx, k)[:, ::-1], ini, ALU.mult, ALU.add))(),
                  deps=[("si", qc)], inc=("dve", 1), label=("scan", qc, k))
            E("vector", (lambda: lambda e: e.tensor_copy(
                carry[:].rearrange("p (k n) -> p k n", k=KD),
                thx[:].rearrange("p (k n) -> p k n", k=KD)[:, :, 0:64]))(),
              deps=[("scan", qc, 7)], inc=("dve", 1), label=("cv", qc))


    BAR = [("#", "pe", pl.cnt["pe"]), ("#", "act", pl.cnt["act"]),
           ("#", "dve", pl.cnt["dve"]), ("#", "gp", pl.cnt["gp"]),
           ("#", "sxg", pl.cnt["sxg"]),
           ("#", "sxnb", pl.cnt["sxnb"]), ("#", "sa1", pl.cnt["sa1"]),
           ("#", "xld", pl.cnt["xld"])]

    psF = (pb[0], pb[1])
    psU = (pb[2], pb[3])
    psN = (pb[4], pb[5])
    psP = (pb[6], pb[7])

    for nm, src in (("wf1", None), ("wf2", None)):
        for k in range(KD):
            E("gpsimd", (lambda nm=nm, k=k: lambda e: e.dma_start(
                out=wP2[nm][:, k, :], in_=w_r[nm][k]))(),
              deps=BAR, inc=("g2", 16), label=("w2", nm, k))
    for k in range(KD):
        E("gpsimd", (lambda k=k: lambda e: e.dma_start(
            out=wP2["wc"][:, k, :], in_=wc_h[:, k, :]))(),
          deps=BAR + [("#", "gsp", 32)], inc=("g2", 16), label=("w2", "wc", k))
    G2TOT = ("#", "g2", pl.cnt["g2"])
    E("sync", lambda e: e.dma_start(out=afw[:], in_=afwbc_h),
      deps=BAR, inc=("x2", 16), label="afb")

    for i in range(-2, NCH + 1):
        jl = i + 1
        if vq(jl):
            E("sync", (lambda jl=jl: lambda e: e.dma_start(
                out=xgin[jl % 2][:], in_=xg_h[:, jl, :]))(),
              deps=BAR + [("U", jl - 2, 7), ("p2xg", jl - 1)],
              inc=("p2g", 16), label=("p2xg", jl))
            E("sync", (lambda jl=jl: lambda e: e.dma_start(
                out=xnbin[jl % 2][:], in_=xnb_h[:, jl, :]))(),
              deps=BAR + [("FF", jl - 2, 7), ("preA", jl - 2), ("p2xnb", jl - 1)],
              inc=("p2n", 16), label=("p2xnb", jl))

        jf = i - 1
        if vq(jf):
            for m in range(KD):
                fns = [(lambda jf=jf, m=m, kk=kk: lambda e: e.matmul(
                    psF[m % 2][:], wP2["wf2"][:, kk, m * P:(m + 1) * P],
                    ksl(hfb, kk), start=(kk == 0), stop=(kk == KD - 1)))()
                    for kk in range(KD)]
                E("tensor", fns,
                  deps=BAR + [("hf", jf), ("ssum", jf, m - 2),
                              ("ssum", jf - 1, 6 + (m % 2)), G2TOT],
                  inc=("pe", 1), label=("OF", jf, m))
                E("vector", (lambda jf=jf, m=m: lambda e: e.scalar_tensor_tensor(
                    ksl(a1in[jf % 2], m), psF[m % 2][:], pk(BF2, m),
                    ksl(a1in[jf % 2], m), ALU.add, ALU.add))(),
                  deps=[("OF", jf, m), ("preA", jf)],
                  inc=("dve", 1), label=("ssum", jf, m))
                E("gpsimd", (lambda jf=jf, m=m: lambda e: e.tensor_tensor(
                    ksl(a1in[jf % 2], m), ksl(a1in[jf % 2], m),
                    ksl(gbuf, m), ALU.mult))(),
                  deps=[("ssum", jf, m), ("gev", jf, m)],
                  inc=("gp", 1), label=("prod", jf, m))
            for g in range(KD):
                jj, h = divmod(g, 2)
                fns = [(lambda jf=jf, jj=jj, h=h, kk=kk: lambda e: e.transpose(
                    psP[(jj * 2 + h) % 2][:, kk * P:(kk + 1) * P],
                    a1in[jf % 2][:, (h * 4 + kk) * SC + jj * P:
                                 (h * 4 + kk) * SC + (jj + 1) * P],
                    ident[:]))() for kk in list(range(4)) + [3]]
                E("tensor", fns,
                  deps=BAR + [("prod", jf, h * 4 + 3), ("oadd", jf, g - 2),
                              ("oadd", jf - 1, 6 + (g % 2))],
                  inc=("pe", 1), label=("Tp", jf, g))
                E("vector", (lambda jf=jf, jj=jj, h=h, g=g: lambda e:
                             e.tensor_tensor(
                                 xin[jf % 2][:, jj, h * SC:(h + 1) * SC],
                                 xin[jf % 2][:, jj, h * SC:(h + 1) * SC],
                                 psP[g % 2][:], ALU.add))(),
                  deps=[("Tp", jf, g), ("p2x", jf)],
                  inc=("dve", 1), label=("oadd", jf, g))
            E("scalar", (lambda jf=jf: lambda e: e.dma_start(
                out=y_r[jf], in_=xin[jf % 2][:]))(),
              deps=[("oadd", jf, 7), ("yst", jf - 1)],
              inc=("yst", 16), label=("yst", jf))

        jc = i
        if vq(jc):
            for m in range(KD):
                fns = [(lambda jc=jc, m=m, kk=kk: lambda e: e.matmul(
                    psU[m % 2][:], wP2["wf1"][:, kk, m * P:(m + 1) * P],
                    ksl(xgin[jc % 2], kk), start=(kk == 0), stop=(kk == KD - 1)))()
                    for kk in range(KD)]
                E("tensor", fns,
                  deps=BAR + [("p2xg", jc), ("uev", jc, m - 2),
                              ("uev", jc - 1, 6 + (m % 2)), G2TOT],
                  inc=("pe", 1), label=("U", jc, m))
                E("scalar", (lambda jc=jc, m=m: lambda e: e.activation(
                    ksl(ubuf, m), psU[m % 2][:], AF.Identity, bias=pk(BF1, m)))(),
                  deps=[("U", jc, m), ("p2cv", jc - 1), ("hf", jc - 1)],
                  inc=("act", 1), label=("uev", jc, m))
            for m in range(KD):
                fns = [(lambda jc=jc, m=m, kk=kk: lambda e: e.matmul(
                    psN[m % 2][:], wP2["wc"][:, kk, m * P:(m + 1) * P],
                    ksl(xnbin[jc % 2], kk), start=(kk == 0), stop=(kk == KD - 1)))()
                    for kk in range(KD)]
                E("tensor", fns,
                  deps=BAR + [("p2xnb", jc), ("gev", jc, m - 2),
                              ("gev", jc - 1, 6 + (m % 2)), G2TOT],
                  inc=("pe", 1), label=("FF", jc, m))
                E("scalar", (lambda jc=jc, m=m: lambda e: e.activation(
                    ksl(gbuf, m), psN[m % 2][:], AF.Silu, bias=pk(BC, m)))(),
                  deps=[("FF", jc, m), ("prod", jc - 1, m)],
                  inc=("act", 1), label=("gev", jc, m))
            E("gpsimd", (lambda jc=jc: lambda e: e.tensor_tensor(
                a1in[jc % 2][:], a1in[jc % 2][:], xnbin[jc % 2][:], ALU.add))(),
              deps=[("p2a1", jc), ("p2xnb", jc)],
              inc=("gp", 1), label=("preA", jc))
            for k in range(KD):
                ini = 0.0 if jc == 0 else carry[:, k * 64 + 63:k * 64 + 64]
                E("vector", (lambda k=k, ini=ini, jc=jc: lambda e:
                             e.tensor_tensor_scan(
                                 ksl(ubuf, k), ksl(afw, k), ksl(ubuf, k),
                                 ini, ALU.mult, ALU.add))(),
                  deps=[("uev", jc, 7), "afb"],
                  inc=("dve", 1), label=("p2scan", jc, k))
            E("vector", (lambda: lambda e: e.tensor_copy(
                carry[:].rearrange("p (k n) -> p k n", k=KD),
                ubuf[:].rearrange("p (k n) -> p k n", k=KD)[:, :, SC - 64:SC]))(),
              deps=[("p2scan", jc, 7)], inc=("dve", 1), label=("p2cv", jc))
            E("scalar", (lambda: lambda e: e.activation(
                hfb[:], ubuf[:], AF.Identity))(),
              deps=[("p2scan", jc, 7), ("OF", jc - 1, 7)],
              inc=("act", 1), label=("hf", jc))
        if vq(jl):
            E("sync", (lambda jl=jl: lambda e: e.dma_start(
                out=xin[jl % 2][:], in_=x_r[jl]))(),
              deps=BAR + [("yst", jl - 2), ("p2x", jl - 1), "afb"],
              inc=("x2", 16), label=("p2x", jl))
            E("sync", (lambda jl=jl: lambda e: e.dma_start(
                out=a1in[jl % 2][:], in_=a1_h[:, jl, :]))(),
              deps=BAR + [("Tp", jl - 2, 7), ("p2a1", jl - 1)],
              inc=("p2a", 16), label=("p2a1", jl))

    fwd = [d for d in pl.skipped if d in pl.ev]
    assert not fwd, f"forward-referenced deps (ordering bugs): {sorted(set(fwd))[:20]}"

    E("sync", lambda e: e.nop(), deps=[("#", "yst", pl.cnt["yst"])], inc=None)

    es = contextlib.ExitStack()
    with es:
        sems = {s: es.enter_context(nc.semaphore(f"sem_{s}")) for s in SEMS}
        block = es.enter_context(nc.Block())

        @block.sync
        def _(eng):
            pl.run("sync", eng, sems)

        @block.scalar
        def _(eng):
            pl.run("scalar", eng, sems)

        @block.vector
        def _(eng):
            pl.run("vector", eng, sems)

        @block.tensor
        def _(eng):
            pl.run("tensor", eng, sems)

        @block.gpsimd
        def _(eng):
            pl.run("gpsimd", eng, sems)

    return nc


def kernel(**inputs):
    x = np.asarray(inputs["L0_attn_x"], dtype=np.float32)
    vecs = [np.asarray(inputs[k], dtype=np.float32) for k in (
        "L0_attn_bf1", "L0_attn_bf2", "L0_attn_bbx", "L0_attn_bgx",
        "L0_attn_bga", "L0_attn_bb2", "L0_ffn_b", "L0_attn_a_fwd",
        "L0_attn_a_param")]
    packs = np.concatenate([v.reshape(KD, P).T for v in vecs], axis=1)
    packs = np.ascontiguousarray(packs, dtype=np.float32)
    base = {
        "wf1": np.ascontiguousarray(inputs["L0_attn_Wf1"], np.float32),
        "wf2": np.ascontiguousarray(inputs["L0_attn_Wf2"], np.float32),
        "wbx": np.ascontiguousarray(inputs["L0_attn_Wbx"], np.float32),
        "wgx": np.ascontiguousarray(inputs["L0_attn_Wgx"], np.float32),
        "wga": np.ascontiguousarray(inputs["L0_attn_Wga"], np.float32),
        "wb2": np.ascontiguousarray(inputs["L0_attn_Wb2"], np.float32),
        "wT": np.ascontiguousarray(np.asarray(inputs["L0_ffn_W"]).T, np.float32),
        "dna": np.ascontiguousarray(inputs["L0_ffn_down_aux"], np.float32),
        "uaT": np.ascontiguousarray(np.asarray(inputs["L0_ffn_up_aux"]).T,
                                    np.float32),
        "ldT": np.ascontiguousarray(np.asarray(inputs["L0_ffn_lora_down"]).T,
                                    np.float32),
        "luT": np.ascontiguousarray(np.asarray(inputs["L0_ffn_lora_up"]).T,
                                    np.float32),
        "packs": packs,
        "ident": np.eye(P, dtype=np.float32),
        "afwbc": np.ascontiguousarray(np.repeat(
            np.asarray(inputs["L0_attn_a_fwd"], np.float32).reshape(KD, P).T
            [:, :, None], SC, axis=2).reshape(P, KD * SC)),
    }
    nc = build_nc()
    in_maps = []
    for b in range(8):
        m = dict(base)
        m["x"] = np.ascontiguousarray(x[b], np.float32)
        in_maps.append(m)
    res = run_bass_kernel_spmd(nc, in_maps, core_ids=list(range(8)))
    return np.stack([r["y"] for r in res.results], axis=0)
